# version 02
# baseline (speedup 1.0000x reference)
import sys
import numpy as np
import ml_dtypes
from contextlib import ExitStack

import concourse.bass as bass
import concourse.mybir as mybir
from concourse.bass_utils import run_bass_kernel_spmd

F32 = mybir.dt.float32
BF16 = mybir.dt.bfloat16
FP16 = mybir.dt.float16
AF = mybir.ActivationFunctionType
ALU = mybir.AluOpType
AXX = mybir.AxisListType.X

D = 1024
DI = 2048
SEQ = 2048
NSEQ = 2
NST = 16
DTR = 64
LC = 512
NCH = SEQ // LC
NSUB = LC // 128
FC = D // 128
DC = DI // 128
TS = SEQ // 128
EPS = 1e-5
UW = 576
UOF = 64

ENGS = ("tensor", "vector", "scalar", "gpsimd", "sync")
_PHASE_LIMIT = None
_A_STEPS = None
_A_DBG = False
RELAXED = False


class Planner:

    def __init__(self):
        self.ops = []
        self.bufs = {}
        self.groups = {}
        self.nbar = 0
        self.tag = None
        self.last_eng = {}

    def _b(self, k):
        b = self.bufs.get(k)
        if b is None:
            b = {"w": None, "r": []}
            self.bufs[k] = b
        return b

    def _g(self, name):
        g = self.groups.get(name)
        if g is None:
            g = {"gen": 0, "count": 0, "bounds": [], "pending": set(), "last": None}
            self.groups[name] = g
        return g

    def add(self, eng, fn, r=(), w=(), grp=None, extra=(), bar_inc=False,
            bar_wait=None, async_out=False):
        fr = sys._getframe(1)
        tag = None
        for _ in range(6):
            if fr is None:
                break
            nm = fr.f_code.co_name
            if nm.startswith("phase") or nm in ("prep", "build", "ln_stats",
                                                "ln_math", "combine_halves",
                                                "barrier"):
                tag = (nm, fr.f_lineno,
                       tuple((k, fr.f_locals[k]) for k in
                             ("s", "c", "sub", "n", "dc", "oc", "fc", "half",
                              "qb", "qs", "g", "dcb", "kc")
                             if k in fr.f_locals and isinstance(fr.f_locals[k], int)))
                break
            fr = fr.f_back
        self.tag = tag
        i = len(self.ops)
        deps = set(extra)
        sdeps = set()
        for k in r:
            b = self._b(k)
            if b["w"] is not None:
                deps.add(b["w"])
        for k in w:
            b = self._b(k)
            if b["w"] is not None:
                sdeps.add(b["w"])
            sdeps.update(b["r"])
        deps.discard(i)
        sdeps.discard(i)
        deps -= sdeps
        gw = {}
        eng_deps = set()
        dropped = set()
        for j in deps | sdeps:
            oj = self.ops[j]
            if oj["grp"] is not None:
                gname, gen = oj["grp"]
                gw[gname] = max(gw.get(gname, -1), gen)
                gg = self._g(gname)
                if gen == gg["gen"]:
                    gg["pending"].add(i)
            else:
                eng_deps.add(j)
        gtag = None
        if grp is not None:
            g = self._g(grp)
            pend = {j for j in g["pending"] if j != i}
            if pend:
                g["bounds"].append(g["count"])
                g["gen"] += 1
                eng_deps.update(j for j in pend if self.ops[j]["grp"] is None)
                for j in pend:
                    oj = self.ops[j]
                    if oj["grp"] is not None:
                        gn, gg = oj["grp"]
                        gw[gn] = max(gw.get(gn, -1), gg)
                g["pending"] = set()
            g["count"] += 1
            g["last"] = i
            gtag = (grp, g["gen"])
        self.ops.append({"eng": eng, "fn": fn, "grp": gtag, "deps": eng_deps,
                         "gw": gw, "dropped": dropped, "tag": self.tag,
                         "async_out": async_out,
                         "bar_inc": bar_inc, "bar_wait": bar_wait})
        for k in r:
            self._b(k)["r"].append(i)
        for k in w:
            b = self._b(k)
            b["w"] = i
            b["r"] = []
        if gtag is None and not bar_inc and bar_wait is None:
            self.last_eng[eng] = i
        return i

    def barrier(self):
        lasts = [i for i in self.last_eng.values()]
        last_dmas = [g["last"] for g in self.groups.values() if g["last"] is not None]
        for g in self.groups.values():
            done = g["bounds"][-1] if g["bounds"] else 0
            if g["count"] > done:
                g["bounds"].append(g["count"])
                g["gen"] += 1
        self.add("sync", lambda eng: eng.nop(), extra=lasts + last_dmas,
                 bar_inc=True)
        self.nbar += 1
        for e in ENGS:
            if e != "sync":
                self.add(e, lambda eng: eng.nop(), bar_wait=self.nbar)

    def finalize(self):
        n = len(self.ops)
        for g in self.groups.values():
            g["bounds"].append(g["count"])
        if RELAXED:
            for o in self.ops:
                if o["grp"] is None:
                    o["deps"] = {j for j in o["deps"]
                                 if self.ops[j]["eng"] != o["eng"]
                                 or self.ops[j]["grp"] is not None
                                 or self.ops[j]["async_out"]}
            need = [False] * n
            for o in self.ops:
                for j in o["deps"]:
                    need[j] = True
        else:
            need = [o["grp"] is None and not o["bar_inc"] and o["bar_wait"] is None
                    for o in self.ops]
        cum = {}
        val = [0] * n
        for i, o in enumerate(self.ops):
            if o["grp"] is not None:
                continue
            if need[i]:
                s = "e_" + o["eng"]
                cum[s] = cum.get(s, 0) + 1
                val[i] = cum[s]

        def gen_threshold(gname, gen):
            b = self.groups[gname]["bounds"]
            k = min(gen, len(b) - 1)
            return 16 * b[k]

        sched = {k: [] for k in ENGS}
        floor = {e: {} for e in ENGS}
        for i, o in enumerate(self.ops):
            wmap = {}
            for j in o["deps"]:
                if val[j] > 0:
                    s = "e_" + self.ops[j]["eng"]
                    wmap[s] = max(wmap.get(s, 0), val[j])
            for gname, gen in o["gw"].items():
                s = "g_" + gname
                wmap[s] = max(wmap.get(s, 0), gen_threshold(gname, gen))
            if o["bar_wait"] is not None:
                wmap["bar"] = o["bar_wait"]
            if o["bar_inc"]:
                inc = ("bar", 1)
                nbc = self.bar_cnt = getattr(self, "bar_cnt", 0) + 1
                if nbc > 1:
                    wmap["bar"] = max(wmap.get("bar", 0), nbc - 1)
            elif o["grp"] is not None:
                inc = ("g_" + o["grp"][0], 16)
            elif need[i]:
                inc = ("e_" + o["eng"], 1)
                if not RELAXED:
                    s = inc[0]
                    if val[i] > 1:
                        wmap[s] = max(wmap.get(s, 0), val[i] - 1)
            else:
                inc = None
            fl = floor[o["eng"]]
            wl = []
            for s, v in sorted(wmap.items()):
                if fl.get(s, -1) < v:
                    fl[s] = v
                    wl.append((s, v))
            sched[o["eng"]].append((o["fn"], wl, inc, o["tag"]))
        return sched, cum


class Builder:
    def __init__(self):
        self.nc = bass.Bass("TRN2", target_bir_lowering=False,
                            detect_race_conditions=not RELAXED)
        self.pl = Planner()
        self.ctx = ExitStack()
        self.arena_off = 0

    def sb(self, name, shape, dt):
        return self.ctx.enter_context(self.nc.sbuf_tensor(name, shape, dt))

    def view(self, dt, cols):
        esz = 2 if dt in (BF16, FP16) else 4
        off = (self.arena_off + 63) & ~63
        self.arena_off = off + cols * esz
        assert self.arena_off <= self.ARENA_BYTES, \
            f"arena overflow: {self.arena_off} > {self.ARENA_BYTES}"
        base = self.arena.ap() if dt == BF16 else self.arena.ap().bitcast(dt)
        return base[:, off // esz: off // esz + cols]

    def reset_arena(self):
        self.arena_off = 0

    def dma(self, q, out, in_, grp, r=(), w=()):
        self.pl.add(q, lambda e, o=out, i=in_: e.dma_start(out=o, in_=i),
                    r=r, w=w, grp=grp)

    def dmat(self, out, in_, r=(), w=()):
        self.pl.add("scalar",
                    lambda e, o=out, i=in_: e.dma_start_transpose(out=o, in_=i),
                    r=r, w=w, grp="tp")

    def mm(self, out, lhsT, rhs, start, stop, r=(), w=()):
        self.pl.add("tensor",
                    lambda e, o=out, l=lhsT, rr=rhs, s1=start, s2=stop:
                    e.matmul(o, l, rr, start=s1, stop=s2), r=r, w=w)

    def tpose(self, out, in_, r=(), w=()):
        self.pl.add("tensor",
                    lambda e, o=out, i=in_, d=self.IDEN[:]: e.transpose(o, i, d),
                    r=r, w=w)

    def act(self, out, in_, func, bias=0.0, scale=1.0, accum=None, r=(), w=()):
        self.pl.add("scalar",
                    lambda e, o=out, i=in_, f=func, b=bias, s=scale, a=accum:
                    e.activation(out=o, in_=i, func=f, bias=b, scale=s, accum_out=a),
                    r=r, w=w, async_out=accum is not None)

    def tt(self, out, a, b, op, r=(), w=()):
        self.pl.add("vector",
                    lambda e, o=out, x=a, y=b, p=op:
                    e.tensor_tensor(out=o, in0=x, in1=y, op=p), r=r, w=w)

    def ts(self, out, a, s1, s2, op0, op1, accum=None, r=(), w=()):
        if s2 is None and accum is None:
            self.pl.add("vector",
                        lambda e, o=out, x=a, u=s1, p0=op0:
                        e.tensor_scalar(out=o, in0=x, scalar1=u, scalar2=None, op0=p0),
                        r=r, w=w)
        else:
            self.pl.add("vector",
                        lambda e, o=out, x=a, u=s1, v=s2, p0=op0, p1=op1, ac=accum:
                        e.tensor_scalar(out=o, in0=x, scalar1=u, scalar2=v,
                                        op0=p0, op1=p1, accum_out=ac), r=r, w=w,
                        async_out=accum is not None)

    def stt(self, out, in0, scal, in1, op0, op1, r=(), w=()):
        self.pl.add("vector",
                    lambda e, o=out, x=in0, u=scal, y=in1, p0=op0, p1=op1:
                    e.scalar_tensor_tensor(out=o, in0=x, scalar=u, in1=y,
                                           op0=p0, op1=p1), r=r, w=w)

    def scan(self, out, a, b, init, r=(), w=()):
        self.pl.add("vector",
                    lambda e, o=out, x=a, y=b, h0=init:
                    e.tensor_tensor_scan(out=o, data0=x, data1=y, initial=h0,
                                         op0=ALU.mult, op1=ALU.add), r=r, w=w)

    def vcopy(self, out, in_, r=(), w=()):
        self.pl.add("vector", lambda e, o=out, i=in_: e.tensor_copy(out=o, in_=i),
                    r=r, w=w)

    def gcopy(self, out, in_, r=(), w=()):
        self.pl.add("gpsimd", lambda e, o=out, i=in_: e.tensor_copy(out=o, in_=i),
                    r=r, w=w)

    def vmemset(self, ap, val, r=(), w=()):
        self.pl.add("vector", lambda e, a=ap, v=val: e.memset(a, v), r=r, w=w)

    def rmax(self, out, in_, negate, r=(), w=()):
        self.pl.add("vector",
                    lambda e, o=out, i=in_, ng=negate:
                    e.reduce_max(out=o, in_=i, axis=AXX, negate=ng), r=r, w=w)

    def rsumf(self, out, in_, r=(), w=()):
        self.pl.add("vector",
                    lambda e, o=out, i=in_: e.reduce_sum(out=o, in_=i, axis=AXX),
                    r=r, w=w)

    def recip(self, out, in_, r=(), w=()):
        self.pl.add("vector", lambda e, o=out, i=in_: e.reciprocal(out=o, in_=i),
                    r=r, w=w)

    def build(self):
        nc = self.nc
        self.x_in = nc.dram_tensor("x", [NSEQ, SEQ, D], F32, kind="ExternalInput").ap()
        self.out = nc.dram_tensor("out", [NSEQ, SEQ, D], F32, kind="ExternalOutput").ap()
        f32in = lambda n, sh: nc.dram_tensor(n, sh, F32, kind="ExternalInput").ap()
        bf16in = lambda n, sh: nc.dram_tensor(n, sh, BF16, kind="ExternalInput").ap()
        self.w_in = f32in("w_in", [2 * DC, 128, 8 * 128])
        self.conv_w_r = f32in("conv_w_r", [128, 64])
        self.conv_b_r = f32in("conv_b_r", [128, 16])
        self.b_dt_r = f32in("b_dt_r", [128, 16])
        self.d_r = f32in("d_r", [128, 16])
        self.a_log_r = f32in("a_log_r", [128, 256])
        self.w_x = f32in("w_x", [128, DC * 96])
        self.w_dt = f32in("w_dt", [DTR, DI])
        self.w_out = f32in("w_out", [DI, D])
        self.w0t = f32in("w0t", [D, D])
        self.gatet = f32in("gatet", [128, FC * 8])
        self.ut = f32in("ut", [128, FC * 128])
        self.v_r = f32in("v_r", [128, D])
        self.wrt = f32in("wrt", [D, D])
        self.wkt = f32in("wkt", [FC, 128, FC * 128])
        self.wvt = f32in("wvt", [D, D])
        self.mur_r = f32in("mur_r", [128, 8])
        self.muk_r = f32in("muk_r", [128, 8])
        self.selb = bf16in("selb", [32, NST * 128])
        self.selc = bf16in("selc", [32, NST * 128])
        self.rep8 = bf16in("rep8", [8, 128])
        self.ones1 = bf16in("ones1", [128, 1])
        self.iden = bf16in("iden", [128, 128])

        self.win_p = nc.dram_tensor("win_p", [2 * DC, 128, 8 * 128], BF16).ap()
        self.wout_p = nc.dram_tensor("wout_p", [DI, D], BF16).ap()
        self.w0t_p = nc.dram_tensor("w0t_p", [D, D], BF16).ap()
        self.wrt_p = nc.dram_tensor("wrt_p", [D, D], BF16).ap()
        self.wkt_p = nc.dram_tensor("wkt_p", [FC, 128, FC * 128], BF16).ap()
        self.wvt_p = nc.dram_tensor("wvt_p", [D, D], BF16).ap()

        self.ARENA_BYTES = 108 * 1024
        self.arena = self.sb("arena", [128, self.ARENA_BYTES // 2], BF16)
        self.XRES = self.sb("xres", [128, TS * D], F32)
        self.WSTR = self.sb("wstr", [128, 2 * 1024], BF16)
        self.WX = self.sb("wxr", [128, DC * 96], BF16)
        self.WDT = self.sb("wdtr", [DTR, DI], BF16)
        self.UTS = self.sb("uts", [128, FC * 128], BF16)
        self.VSB = self.sb("vsb", [128, D], BF16)
        self.GATET = self.sb("gatetr", [128, FC * 8], BF16)
        self.SELB = self.sb("selbr", [128, NST * 128], BF16)
        self.SELC = self.sb("selcr", [128, NST * 128], BF16)
        self.REP8 = self.sb("rep8r", [8, 128], BF16)
        self.ONES1 = self.sb("ones1r", [128, 1], BF16)
        self.IDEN = self.sb("idenr", [128, 128], BF16)
        self.CONVW = self.sb("convw", [128, 64], F32)
        self.CONVB = self.sb("convb", [128, 16], F32)
        self.BDT = self.sb("bdt", [128, 16], F32)
        self.DCOL = self.sb("dcol", [128, 16], F32)
        self.ANEG = self.sb("aneg", [128, 256], F32)
        self.MUR = self.sb("mur", [128, 8], F32)
        self.MUK = self.sb("muk", [128, 8], F32)
        self.HST = self.sb("hst", [128, 256], F32)
        self.S1 = self.sb("s1", [128, 16], F32)
        self.S2 = self.sb("s2", [128, 16], F32)
        self.SC = self.sb("sc", [128, 16], F32)
        self.SD = self.sb("sd", [128, 16], F32)
        self.SA = self.sb("sa", [128, 16], F32)
        self.SB_ = self.sb("sbb", [128, 16], F32)
        self.NEGM = self.sb("negm", [128, 16], F32)
        self.RSTD = self.sb("rstd", [128, 16], F32)
        self.AUX = self.sb("aux", [128, 16], F32)
        self.AUX2 = self.sb("aux2", [128, 16], F32)
        self.SV = self.sb("sv", [128, 16], F32)
        self.SW = self.sb("sw", [128, 16], F32)
        self.RINV = self.sb("rinv", [128, 4], F32)
        self.NM = self.sb("nm", [128, 1], F32)
        self.GE = self.sb("ge", [128, 8], F32)
        self.GS = self.sb("gs", [128, 2], F32)
        self.GSUB = self.sb("gsub", [128, 8], BF16)
        self.GFM = self.sb("gfm", [8, LC], BF16)
        self.DUMPV = self.sb("dumpv", [128, 1024], BF16)
        self.DUMPA = self.sb("dumpa", [128, 1024], BF16)
        self.FCONST = self.sb("fconst", [128, 2], F32)
        self.nc.const_aps.register(F32, EPS, self.FCONST[:, 0:1])
        self.nc.const_aps.register(F32, 1.0, self.FCONST[:, 1:2])
        self.PS = [self.ctx.enter_context(self.nc.psum_tensor(f"ps{i}", [128, 512], F32))
                   for i in range(8)]

        self.prep()
        phases = []
        for s in range(NSEQ):
            phases += [(self.phaseA, s), (self.phaseB, s),
                       (self.phaseC, s), (self.phaseD, s)]
        if _PHASE_LIMIT is not None:
            phases = phases[:_PHASE_LIMIT]
        for fn, s in phases:
            self.pl.barrier()
            self.reset_arena()
            fn(s)
        if _PHASE_LIMIT is not None and _PHASE_LIMIT > 0 and not _A_DBG:
            self.pl.barrier()
            for g in range(TS):
                self.dma("sync", self.out[0, g * 128:(g + 1) * 128, :],
                         self.XRES[:, g * D:(g + 1) * D], "out",
                         r=[("xres", g)], w=[("out", 0, g)])

        self.pl.add("sync", lambda e: e.nop(),
                    r=[("out", s, g) for s in range(NSEQ) for g in range(TS)] +
                      [(k, dc) for k in ("dbgu", "dbgd", "dbgy")
                       for dc in range(DC)] + [("dbgx", fc) for fc in range(FC)] +
                      [("dbgs", i) for i in range(10)])

        sched, cum = self.pl.finalize()
        sems = {}
        names = ["e_tensor", "e_vector", "e_scalar", "e_gpsimd", "e_sync", "bar"]
        names += ["g_" + g for g in self.pl.groups]
        for name in names:
            sems[name] = self.ctx.enter_context(nc.semaphore(name))

        self.itag = {}

        def body(eng_obj, items):
            for fn, wl, inc, tag in items:
                for sname, v in wl:
                    wi = eng_obj.wait_ge(sems[sname], v)
                    self.itag[wi.ins.name] = ("W", tag)
                inst = fn(eng_obj)
                if inst is not None:
                    self.itag[inst.ins.name] = tag
                if inc is not None:
                    inst.then_inc(sems[inc[0]], inc[1])

        with nc.Block() as blk:
            blk.tensor(lambda e: body(e, sched["tensor"]))
            blk.vector(lambda e: body(e, sched["vector"]))
            blk.scalar(lambda e: body(e, sched["scalar"]))
            blk.gpsimd(lambda e: body(e, sched["gpsimd"]))
            blk.sync(lambda e: body(e, sched["sync"]))
        self.ctx.close()
        return nc

    def prep(self):
        self.vmemset(self.FCONST[:, 0:1], EPS, w=[("fconst",)])
        self.vmemset(self.FCONST[:, 1:2], 1.0, w=[("fconst",)])
        for dst, src, key in [(self.win_p, self.w_in, "win_p"),
                              (self.wout_p, self.w_out, "wout_p"),
                              (self.w0t_p, self.w0t, "w0t_p"),
                              (self.wrt_p, self.wrt, "wrt_p"),
                              (self.wkt_p, self.wkt, "wkt_p"),
                              (self.wvt_p, self.wvt, "wvt_p")]:
            rows = src.shape[0]
            q = rows // 4
            for j in range(4):
                self.dma("gpsimd", dst[j * q:(j + 1) * q], src[j * q:(j + 1) * q],
                         "p_" + key, w=[(key, j)])
        self.dma("gpsimd", self.WX[:],
                 self.w_x[:], "pres", w=[("wx",)])
        self.dma("gpsimd", self.WDT[:], self.w_dt[:], "pres", w=[("wdt",)])
        self.dma("gpsimd", self.UTS[:],
                 self.ut[:], "pres", w=[("uts",)])
        self.dma("gpsimd", self.VSB[:], self.v_r[:], "pres", w=[("vsb",)])
        self.dma("gpsimd", self.GATET[:],
                 self.gatet[:], "pres", w=[("gatet",)])
        for dst, src, key in [(self.CONVW, self.conv_w_r, "convw"),
                              (self.CONVB, self.conv_b_r, "convb"),
                              (self.BDT, self.b_dt_r, "bdt"),
                              (self.DCOL, self.d_r, "dcol"),
                              (self.MUR, self.mur_r, "mur"),
                              (self.MUK, self.muk_r, "muk")]:
            self.dma("sync", dst[:], src[:], "psml", w=[(key,)])
        self.dma("sync", self.SELB[64:96, :], self.selb[:], "pcon", w=[("selb",)])
        self.dma("sync", self.SELC[64:96, :], self.selc[:], "pcon", w=[("selc",)])
        self.dma("sync", self.REP8[:], self.rep8[:], "pcon", w=[("rep8",)])
        self.dma("sync", self.ONES1[:], self.ones1[:], "pcon", w=[("ones1",)])
        self.dma("sync", self.IDEN[:], self.iden[:], "pcon", w=[("iden",)])
        self.dma("sync", self.ANEG[:], self.a_log_r[:], "pan", w=[("aneg",)])
        self.act(self.ANEG[:], self.ANEG[:], AF.Exp, r=[("aneg",)], w=[("aneg",)])
        self.ts(self.ANEG[:], self.ANEG[:], -1.0, None, ALU.mult, None,
                r=[("aneg",)], w=[("aneg",)])

    def ln_stats(self, src_ap, col, key_r, pf, s1="s1", s2="s2"):
        t1 = self.S1 if s1 == "s1" else self.SC
        t2 = self.S2 if s2 == "s2" else self.SD
        self.ts(self.DUMPV[:, 0:src_ap.shape[1]], src_ap, 1.0, 0.0, ALU.mult, ALU.add,
                accum=t1[:, col:col + 1],
                r=key_r, w=[(pf, s1, col), (pf, "dumpv")])
        self.act(self.DUMPA[:, 0:src_ap.shape[1]], src_ap, AF.Square,
                 accum=t2[:, col:col + 1],
                 r=key_r, w=[(pf, s2, col), (pf, "dumpa")])

    def ln_math(self, lo, n, pf, want_r=False):
        sl = slice(lo, lo + n)
        rs1 = [(pf, "s1", c) for c in range(lo, lo + n)]
        rs2 = [(pf, "s2", c) for c in range(lo, lo + n)]
        self.ts(self.NEGM[:, sl], self.S1[:, sl], -1.0 / D, None, ALU.mult, None,
                r=rs1, w=[(pf, "negm")])
        self.ts(self.SA[:, sl], self.S1[:, sl], 1.0 / D, None, ALU.mult, None,
                r=rs1, w=[(pf, "sa")])
        self.tt(self.SB_[:, sl], self.SA[:, sl], self.SA[:, sl], ALU.mult,
                r=[(pf, "sa")], w=[(pf, "sb")])
        self.stt(self.SV[:, sl], self.S2[:, sl], 1.0 / D, self.SB_[:, sl],
                 ALU.mult, ALU.subtract, r=rs2 + [(pf, "sb")], w=[(pf, "sv")])
        self.act(self.SW[:, sl], self.SV[:, sl], AF.Ln, bias=EPS,
                 r=[(pf, "sv")], w=[(pf, "sw")])
        self.act(self.RSTD[:, sl], self.SW[:, sl], AF.Exp, scale=-0.5,
                 r=[(pf, "sw")], w=[(pf, "rstd")])
        if want_r:
            self.tt(self.AUX[:, sl], self.RSTD[:, sl], self.RSTD[:, sl], ALU.mult,
                    r=[(pf, "rstd")], w=[(pf, "aux")])
            self.stt(self.AUX[:, sl], self.SV[:, sl], float(D), self.AUX[:, sl],
                     ALU.mult, ALU.mult, r=[(pf, "sv"), (pf, "aux")], w=[(pf, "aux")])
            self.ts(self.AUX[:, sl], self.AUX[:, sl], 1e-20, None, ALU.max, None,
                    r=[(pf, "aux")], w=[(pf, "aux")])
            self.act(self.AUX[:, sl], self.AUX[:, sl], AF.Ln,
                     r=[(pf, "aux")], w=[(pf, "aux")])
            self.act(self.AUX[:, sl], self.AUX[:, sl], AF.Exp, scale=-0.5,
                     r=[(pf, "aux")], w=[(pf, "aux")])
            self.tt(self.AUX2[:, sl], self.RSTD[:, sl], self.AUX[:, sl], ALU.mult,
                    r=[(pf, "rstd"), (pf, "aux")], w=[(pf, "aux2")])

    def combine_halves(self, pf):
        self.tt(self.S1[:, 0:4], self.S1[:, 0:4], self.SC[:, 0:4], ALU.add,
                r=[(pf, "s1", i) for i in range(4)] +
                  [(pf, "s1b", i) for i in range(4)],
                w=[(pf, "s1", i) for i in range(4)])
        self.tt(self.S2[:, 0:4], self.S2[:, 0:4], self.SD[:, 0:4], ALU.add,
                r=[(pf, "s2", i) for i in range(4)] +
                  [(pf, "s2b", i) for i in range(4)],
                w=[(pf, "s2", i) for i in range(4)])

    def w_in_block(self, oc):
        return self.win_p[oc]

    def phaseA(self, s):
        pf = ("A", s)
        U = self.view(BF16, DC * UW)
        DEL = self.view(BF16, DC * 512)
        W = self.view(BF16, DC * 512)
        Y = self.view(FP16, DC * 512)
        XFA = self.view(BF16, FC * 512)
        X0 = [self.view(F32, 1024) for _ in range(2)]
        XH = [self.view(BF16, 1024) for _ in range(2)]
        DR = self.view(BF16, 512)
        XDB = self.view(BF16, 512)
        E1 = self.view(F32, 512)
        BBC = self.view(BF16, 512)
        CBC = self.view(BF16, 512)
        A2 = [self.view(FP16, 512) for _ in range(2)]
        B2 = [self.view(BF16, 512) for _ in range(2)]
        H2 = [self.view(BF16, 512) for _ in range(2)]
        YC = [self.view(BF16, 512) for _ in range(2)]
        ACC = self.view(BF16, 512)
        G = [self.view(BF16, 512) for _ in range(2)]
        TMP = [self.view(F32, 512) for _ in range(2)]

        utile = lambda dc: U[:, dc * UW: dc * UW + UW]
        dtile = lambda dc: DEL[:, dc * 512:(dc + 1) * 512]
        wtile = lambda dc: W[:, dc * 512:(dc + 1) * 512]
        ytile = lambda dc: Y[:, dc * 512:(dc + 1) * 512]
        xfa = lambda fc: XFA[:, fc * 512:(fc + 1) * 512]

        self.vmemset(self.HST[:], 0.0,
                     w=[("hst", n, dc) for n in range(NST) for dc in range(DC)])
        u3 = U.rearrange("p (dc r) -> p dc r", dc=DC)
        self.vmemset(u3[:, :, UOF - 3:UOF], 0.0,
                     w=[(pf, "tl", dc) for dc in range(DC)])

        for c in range(NCH):
            for sub in range(NSUB):
                g = c * NSUB + sub
                p = sub % 2
                self.dma("sync", X0[p][:], self.x_in[s, g * 128:(g + 1) * 128, :],
                         f"x0{p}", w=[(pf, "x0", p)])
                self.ln_stats(X0[p][:], sub, [(pf, "x0", p)], pf)
            self.ln_math(0, NSUB, pf)
            for sub in range(NSUB):
                g = c * NSUB + sub
                p = sub % 2
                self.dma("sync", X0[p][:], self.x_in[s, g * 128:(g + 1) * 128, :],
                         f"x0{p}", w=[(pf, "x0", p)])
                self.ts(XH[p], X0[p][:], self.NEGM[:, sub:sub + 1],
                        self.RSTD[:, sub:sub + 1], ALU.add, ALU.mult,
                        r=[(pf, "x0", p), (pf, "negm"), (pf, "rstd")],
                        w=[(pf, "xh", p)])
                for fc in range(FC):
                    self.dmat(xfa(fc)[:, sub * 128:(sub + 1) * 128],
                              XH[p][:, fc * 128:(fc + 1) * 128],
                              r=[(pf, "xh", p)], w=[(pf, "xfa", fc, sub)])
            xfar = [(pf, "xfa", fc, sub) for fc in range(FC) for sub in range(NSUB)]

            def dbg_dump():
                if not (_A_DBG and c == 0):
                    return
                st = _A_STEPS if _A_STEPS is not None else 99
                if st < 5:
                    for col, t, keys, tag in [
                            (0, self.S1[:, 0:4], [(pf, "s1", i) for i in range(4)], 0),
                            (16, self.S2[:, 0:4], [(pf, "s2", i) for i in range(4)], 1),
                            (32, self.NEGM[:, 0:4], [(pf, "negm")], 2),
                            (48, self.RSTD[:, 0:4], [(pf, "rstd")], 3),
                            (64, self.SA[:, 0:4], [(pf, "sa")], 6),
                            (80, self.SB_[:, 0:4], [(pf, "sb")], 7),
                            (96, self.SV[:, 0:4], [(pf, "sv")], 8),
                            (112, self.SW[:, 0:4], [(pf, "sw")], 9)]:
                        self.dma("gpsimd", self.out[0, 1280:1408, col:col + 4], t,
                                 "out", r=keys, w=[("dbgs", tag)])
                    self.dma("gpsimd", self.out[0, 1024:1152, :], XH[0],
                             "out", r=[(pf, "xh", 0)], w=[("dbgs", 4)])
                    self.dma("gpsimd", self.out[0, 1152:1280, :], XH[1],
                             "out", r=[(pf, "xh", 1)], w=[("dbgs", 5)])
                for fc in range(FC):
                    self.dma("gpsimd", self.out[0, fc * 128:(fc + 1) * 128, 512:1024],
                             xfa(fc), "out",
                             r=[(pf, "xfa", fc, sub) for sub in range(NSUB)],
                             w=[("dbgx", fc)])
                for dc in range(DC):
                    if st >= 2:
                        self.dma("gpsimd", self.out[1, dc * 128:(dc + 1) * 128, 0:512],
                                 utile(dc)[:, UOF:UOF + 512], "out",
                                 r=[(pf, "u", dc)], w=[("dbgu", dc)])
                    if st >= 4:
                        self.dma("gpsimd", self.out[1, dc * 128:(dc + 1) * 128, 512:1024],
                                 dtile(dc), "out",
                                 r=[(pf, "del", dc)], w=[("dbgd", dc)])
                    if st >= 5:
                        self.dma("gpsimd", self.out[0, dc * 128:(dc + 1) * 128, 0:512],
                                 ytile(dc), "out",
                                 r=[(pf, "y", dc)], w=[("dbgy", dc)])
            if _A_STEPS is not None and _A_STEPS <= 1:
                dbg_dump()
                return

            for oc in range(DC):
                p = oc % 2
                wb = self.WSTR[:, p * 1024:p * 1024 + 1024]
                self.dma("sync", wb, self.w_in_block(oc), f"wb{p}",
                         r=[("win_p", j) for j in range(4)], w=[(pf, "wbd", p)])
                pb = self.PS[p]
                for fc in range(FC):
                    self.mm(pb[:], wb[:, fc * 128:(fc + 1) * 128], xfa(fc),
                            start=(fc == 0), stop=(fc == FC - 1),
                            r=[(pf, "wbd", p)] + xfar, w=[("ps", p)])
                self.vcopy(utile(oc)[:, UOF:UOF + 512], pb[:],
                           r=[("ps", p)], w=[(pf, "u", oc), ("ps", p)])
            if _A_STEPS is not None and _A_STEPS <= 2:
                dbg_dump()
                return
            for dc in range(DC):
                ut = utile(dc)
                self.ts(ACC, ut[:, UOF - 3:UOF + 509], self.CONVW[:, dc * 4:dc * 4 + 1],
                        None, ALU.mult, None,
                        r=[(pf, "u", dc), (pf, "tl", dc), ("convw",)],
                        w=[(pf, "acc")])
                for k in range(1, 4):
                    self.stt(ACC, ut[:, UOF - 3 + k:UOF + 509 + k],
                             self.CONVW[:, dc * 4 + k:dc * 4 + k + 1], ACC,
                             ALU.mult, ALU.add,
                             r=[(pf, "u", dc), (pf, "tl", dc), (pf, "acc"), ("convw",)],
                             w=[(pf, "acc")])
                self.gcopy(ut[:, UOF - 3:UOF], ut[:, UOF + 509:UOF + 512],
                           r=[(pf, "u", dc)], w=[(pf, "tl", dc)])
                self.act(ut[:, UOF:UOF + 512], ACC, AF.Silu,
                         bias=self.CONVB[:, dc:dc + 1],
                         r=[(pf, "acc"), ("convb",)], w=[(pf, "u", dc)])
            if _A_STEPS is not None and _A_STEPS <= 3:
                dbg_dump()
                return
            px = self.PS[2]
            for kc in range(DC):
                self.mm(px[0:96, :], self.WX[:, kc * 96:(kc + 1) * 96],
                        utile(kc)[:, UOF:UOF + 512],
                        start=(kc == 0), stop=(kc == DC - 1),
                        r=[(pf, "u", kc), ("wx",)], w=[("ps", 2)])
            self.act(DR[0:64, :], px[0:64, :], AF.Copy, r=[("ps", 2)], w=[(pf, "dr")])
            self.act(XDB[64:96, :], px[64:96, :], AF.Copy,
                     r=[("ps", 2)], w=[(pf, "xdb"), ("ps", 2)])
            for dc in range(DC):
                pi = 3 if dc % 2 == 0 else 2
                pd = self.PS[pi]
                self.mm(pd[:], self.WDT[:, dc * 128:(dc + 1) * 128], DR[0:64, :],
                        start=True, stop=True,
                        r=[(pf, "dr"), ("wdt",)], w=[("ps", pi)])
                self.act(E1, pd[:], AF.Exp, bias=self.BDT[:, dc:dc + 1],
                         r=[("ps", pi), ("bdt",)], w=[(pf, "e1"), ("ps", pi)])
                self.act(dtile(dc), E1, AF.Ln, bias=1.0,
                         r=[(pf, "e1")], w=[(pf, "del", dc)])
                self.tt(wtile(dc), dtile(dc), utile(dc)[:, UOF:UOF + 512], ALU.mult,
                        r=[(pf, "del", dc), (pf, "u", dc)], w=[(pf, "w", dc)])
                self.ts(ytile(dc), utile(dc)[:, UOF:UOF + 512],
                        self.DCOL[:, dc:dc + 1], None, ALU.mult, None,
                        r=[(pf, "u", dc), ("dcol",)], w=[(pf, "y", dc)])
            if _A_STEPS is not None and _A_STEPS <= 4:
                dbg_dump()
                return
            for n in range(NST):
                self.mm(self.PS[4][:], self.SELB[64:96, n * 128:(n + 1) * 128],
                        XDB[64:96, :], start=True, stop=True,
                        r=[(pf, "xdb"), ("selb",)], w=[("ps", 4)])
                self.act(BBC, self.PS[4][:], AF.Copy,
                         r=[("ps", 4)], w=[(pf, "bbc"), ("ps", 4)])
                self.mm(self.PS[5][:], self.SELC[64:96, n * 128:(n + 1) * 128],
                        XDB[64:96, :], start=True, stop=True,
                        r=[(pf, "xdb"), ("selc",)], w=[("ps", 5)])
                self.vcopy(CBC, self.PS[5][:],
                           r=[("ps", 5)], w=[(pf, "cbc"), ("ps", 5)])
                for dc in range(DC):
                    p = dc % 2
                    col = dc * 16 + n
                    self.act(A2[p], dtile(dc), AF.Exp,
                             scale=self.ANEG[:, col:col + 1],
                             r=[(pf, "del", dc), ("aneg",)], w=[(pf, "a2", p)])
                    self.tt(B2[p], wtile(dc), BBC, ALU.mult,
                            r=[(pf, "w", dc), (pf, "bbc")], w=[(pf, "b2", p)])
                    self.scan(H2[p], A2[p], B2[p], self.HST[:, col:col + 1],
                              r=[(pf, "a2", p), (pf, "b2", p), ("hst", n, dc)],
                              w=[(pf, "h2", p)])
                    self.gcopy(self.HST[:, col:col + 1], H2[p][:, 511:512],
                               r=[(pf, "h2", p)], w=[("hst", n, dc)])
                    self.tt(YC[p], H2[p], CBC, ALU.mult,
                            r=[(pf, "h2", p), (pf, "cbc")], w=[(pf, "yc", p)])
                    self.tt(ytile(dc), ytile(dc), YC[p], ALU.add,
                            r=[(pf, "yc", p), (pf, "y", dc)], w=[(pf, "y", dc)])
            if _A_STEPS is not None and _A_STEPS <= 5:
                dbg_dump()
                return
            for oc in range(DC, 2 * DC):
                p = oc % 2
                dc = oc - DC
                wb = self.WSTR[:, p * 1024:p * 1024 + 1024]
                self.dma("sync", wb, self.w_in_block(oc), f"wb{p}",
                         r=[("win_p", j) for j in range(4)], w=[(pf, "wbd", p)])
                pb = self.PS[p]
                for fc in range(FC):
                    self.mm(pb[:], wb[:, fc * 128:(fc + 1) * 128], xfa(fc),
                            start=(fc == 0), stop=(fc == FC - 1),
                            r=[(pf, "wbd", p)] + xfar, w=[("ps", p)])
                self.act(G[p], pb[:], AF.Silu,
                         r=[("ps", p)], w=[(pf, "g", p), ("ps", p)])
                self.tt(utile(dc)[:, UOF:UOF + 512], ytile(dc), G[p], ALU.mult,
                        r=[(pf, "y", dc), (pf, "g", p), (pf, "u", dc)],
                        w=[(pf, "u", dc)])
            if _A_STEPS is not None and _A_STEPS <= 6:
                dbg_dump()
                return
            cont_flag = False
            for half in range(2):
                for dc in range(DC):
                    p = dc % 2
                    wb = self.WSTR[:, p * 1024:p * 1024 + 512]
                    self.dma("sync", wb,
                             self.wout_p[dc * 128:(dc + 1) * 128,
                                         half * 512:(half + 1) * 512],
                             f"wb{p}",
                             r=[("wout_p", dc // 4)], w=[(pf, "wbd", p)])
                    for sub in range(NSUB):
                        self.mm(self.PS[4 + sub][:],
                                utile(dc)[:, UOF + sub * 128:UOF + (sub + 1) * 128],
                                wb[:, 0:512], start=(dc == 0), stop=(dc == DC - 1),
                                r=[(pf, "wbd", p), (pf, "u", dc)],
                                w=[("ps", 4 + sub)])
                for sub in range(NSUB):
                    g = c * NSUB + sub
                    self.vcopy(self.XRES[:, g * D + half * 512:
                                         g * D + (half + 1) * 512],
                               self.PS[4 + sub][:],
                               r=[("ps", 4 + sub)],
                               w=[("xres", g), ("ps", 4 + sub)])
            for sub in range(NSUB):
                g = c * NSUB + sub
                self.ln_stats(self.XRES[:, g * D:(g + 1) * D], sub,
                              [("xres", g)], pf)
            self.ln_math(0, NSUB, pf)
            for sub in range(NSUB):
                g = c * NSUB + sub
                p = sub % 2
                self.dma("sync", X0[p][:], self.x_in[s, g * 128:(g + 1) * 128, :],
                         f"x0{p}", w=[(pf, "x0", p)])
                for half in range(2):
                    lo = g * D + half * 512
                    self.ts(TMP[half], self.XRES[:, lo:lo + 512],
                            self.NEGM[:, sub:sub + 1], self.RSTD[:, sub:sub + 1],
                            ALU.add, ALU.mult,
                            r=[("xres", g), (pf, "negm"), (pf, "rstd")],
                            w=[(pf, "tmp", half)])
                    self.tt(self.XRES[:, lo:lo + 512],
                            X0[p][:, half * 512:(half + 1) * 512], TMP[half],
                            ALU.add, r=[(pf, "x0", p), (pf, "tmp", half)],
                            w=[("xres", g)])

    def phaseB(self, s):
        pf = ("B", s)
        XFB = self.view(BF16, FC * 512)
        XH = [self.view(BF16, 1024) for _ in range(2)]
        TEXP = self.view(BF16, 512)
        GREP = self.view(BF16, 512)
        ZST = self.view(F32, NSUB * 1024)
        TMP = [self.view(F32, 512) for _ in range(2)]
        xfb = lambda fc: XFB[:, fc * 512:(fc + 1) * 512]

        for c in range(NCH):
            for sub in range(NSUB):
                g = c * NSUB + sub
                self.ln_stats(self.XRES[:, g * D:(g + 1) * D], sub, [("xres", g)], pf)
            self.ln_math(0, NSUB, pf)
            for sub in range(NSUB):
                g = c * NSUB + sub
                p = sub % 2
                self.ts(XH[p], self.XRES[:, g * D:(g + 1) * D],
                        self.NEGM[:, sub:sub + 1], self.RSTD[:, sub:sub + 1],
                        ALU.add, ALU.mult,
                        r=[("xres", g), (pf, "negm"), (pf, "rstd")], w=[(pf, "xh", p)])
                for fc in range(FC):
                    self.dmat(xfb(fc)[:, sub * 128:(sub + 1) * 128],
                              XH[p][:, fc * 128:(fc + 1) * 128],
                              r=[(pf, "xh", p)], w=[(pf, "xfb", fc, sub)])
            xfbr = [(pf, "xfb", fc, sub) for fc in range(FC) for sub in range(NSUB)]

            for fc in range(FC):
                self.mm(self.PS[0][:], self.UTS[:, fc * 128:(fc + 1) * 128], xfb(fc),
                        start=(fc == 0), stop=(fc == FC - 1),
                        r=[("uts",)] + xfbr, w=[("ps", 0)])
            self.vcopy(TEXP, self.PS[0][:], r=[("ps", 0)], w=[(pf, "texp"), ("ps", 0)])

            for sub in range(NSUB):
                for fc in range(FC):
                    self.mm(self.PS[1][:, 0:8],
                            xfb(fc)[:, sub * 128:(sub + 1) * 128],
                            self.GATET[:, fc * 8:(fc + 1) * 8],
                            start=(fc == 0), stop=(fc == FC - 1),
                            r=[("gatet",)] + xfbr, w=[("ps", 1)])
                self.rmax(self.NM[:], self.PS[1][:, 0:8], True,
                          r=[("ps", 1)], w=[(pf, "nm")])
                self.act(self.GE[:], self.PS[1][:, 0:8], AF.Exp, bias=self.NM[:, 0:1],
                         r=[("ps", 1), (pf, "nm")], w=[(pf, "ge"), ("ps", 1)])
                self.rsumf(self.GS[:, 0:1], self.GE[:], r=[(pf, "ge")], w=[(pf, "gs")])
                self.ts(self.GS[:, 1:2], self.GS[:, 0:1], 16.0, None, ALU.mult, None,
                        r=[(pf, "gs")], w=[(pf, "gs16")])
                self.recip(self.GS[:, 0:1], self.GS[:, 1:2],
                           r=[(pf, "gs16")], w=[(pf, "gs")])
                self.ts(self.GSUB[:], self.GE[:], self.GS[:, 0:1], None, ALU.mult,
                        None, r=[(pf, "ge"), (pf, "gs")], w=[(pf, "gsub")])
                pgt = self.PS[2].ap().bitcast(BF16)[0:8, 0:128]
                self.tpose(pgt, self.GSUB[:],
                           r=[(pf, "gsub"), ("iden",)], w=[("ps", 2)])
                self.act(self.GFM[:, sub * 128:(sub + 1) * 128], pgt, AF.Copy,
                         r=[("ps", 2)], w=[(pf, "gfm", sub), ("ps", 2)])
            self.mm(self.PS[3][:], self.REP8[:], self.GFM[:], start=True, stop=True,
                    r=[("rep8",)] + [(pf, "gfm", i) for i in range(NSUB)],
                    w=[("ps", 3)])
            self.vcopy(GREP, self.PS[3][:], r=[("ps", 3)], w=[(pf, "grep"), ("ps", 3)])
            self.tt(TEXP, TEXP, GREP, ALU.mult,
                    r=[(pf, "texp"), (pf, "grep")], w=[(pf, "texp")])

            for half in range(2):
                for fc in range(FC):
                    p = fc % 2
                    wb = self.WSTR[:, p * 1024:p * 1024 + 512]
                    self.dma("sync", wb,
                             self.w0t_p[fc * 128:(fc + 1) * 128,
                                        half * 512:(half + 1) * 512],
                             f"wb{p}",
                             r=[("w0t_p", fc // 2)], w=[(pf, "wbd", p)])
                    for sub in range(NSUB):
                        self.mm(self.PS[4 + sub][:],
                                xfb(fc)[:, sub * 128:(sub + 1) * 128], wb[:, 0:512],
                                start=(fc == 0), stop=False,
                                r=[(pf, "wbd", p)] + xfbr, w=[("ps", 4 + sub)])
                for sub in range(NSUB):
                    self.mm(self.PS[4 + sub][:], TEXP[:, sub * 128:(sub + 1) * 128],
                            self.VSB[:, half * 512:(half + 1) * 512],
                            start=False, stop=True,
                            r=[(pf, "texp"), ("vsb",)], w=[("ps", 4 + sub)])
                for sub in range(NSUB):
                    self.vcopy(ZST[:, sub * 1024 + half * 512:
                                   sub * 1024 + (half + 1) * 512],
                               self.PS[4 + sub][:],
                               r=[("ps", 4 + sub)],
                               w=[(pf, "zst", sub), ("ps", 4 + sub)])
            for sub in range(NSUB):
                self.ln_stats(ZST[:, sub * 1024:(sub + 1) * 1024], sub,
                              [(pf, "zst", sub)], pf)
            self.ln_math(0, NSUB, pf)
            for sub in range(NSUB):
                g = c * NSUB + sub
                for half in range(2):
                    self.ts(TMP[half], ZST[:, sub * 1024 + half * 512:
                                           sub * 1024 + (half + 1) * 512],
                            self.NEGM[:, sub:sub + 1], self.RSTD[:, sub:sub + 1],
                            ALU.add, ALU.mult,
                            r=[(pf, "zst", sub), (pf, "negm"), (pf, "rstd")],
                            w=[(pf, "tmp", half)])
                    self.tt(self.XRES[:, g * D + half * 512:g * D + (half + 1) * 512],
                            self.XRES[:, g * D + half * 512:g * D + (half + 1) * 512],
                            TMP[half], ALU.add,
                            r=[(pf, "tmp", half)], w=[("xres", g)])

    def phaseC(self, s):
        pf = ("C", s)
        XHT = self.view(BF16, TS * D)
        XFC = self.view(BF16, FC * SEQ)
        ES = self.view(BF16, TS * 512)
        AO = [self.view(F32, D) for _ in range(2)]
        XT = [self.view(BF16, D) for _ in range(2)]
        TMP = [self.view(F32, 512) for _ in range(2)]
        xht = lambda g: XHT[:, g * D:(g + 1) * D]
        xfc = lambda fc: XFC[:, fc * SEQ:(fc + 1) * SEQ]
        es = lambda kc: ES[:, kc * 512:(kc + 1) * 512]

        for g in range(TS):
            self.ln_stats(self.XRES[:, g * D:(g + 1) * D], g, [("xres", g)], pf)
        self.ln_math(0, TS, pf, want_r=True)
        for g in range(TS):
            p = g % 2
            self.ts(xht(g), self.XRES[:, g * D:(g + 1) * D],
                    self.NEGM[:, g:g + 1], self.RSTD[:, g:g + 1], ALU.add, ALU.mult,
                    r=[("xres", g), (pf, "negm"), (pf, "rstd")], w=[(pf, "xht", g)])
            self.ts(XT[p], self.XRES[:, g * D:(g + 1) * D],
                    self.NEGM[:, g:g + 1], self.AUX2[:, g:g + 1], ALU.add, ALU.mult,
                    r=[("xres", g), (pf, "negm"), (pf, "aux2")], w=[(pf, "xt", p)])
            for fc in range(FC):
                self.dmat(xfc(fc)[:, g * 128:(g + 1) * 128],
                          XT[p][:, fc * 128:(fc + 1) * 128],
                          r=[(pf, "xt", p)], w=[(pf, "xfc", fc, g)])

        for qb in range(NCH):
            q0 = qb * 512
            xfcq = [(pf, "xfc", fc, qb * 4 + i) for fc in range(FC) for i in range(4)]
            for kc in range(TS):
                pq = kc % 2
                for fc in range(FC):
                    self.mm(self.PS[pq][:], xfc(fc)[:, kc * 128:(kc + 1) * 128],
                            xfc(fc)[:, q0:q0 + 512],
                            start=(fc == 0), stop=(fc == FC - 1),
                            r=[(pf, "xfc", fc, kc)] + xfcq, w=[("ps", pq)])
                self.act(es(kc), self.PS[pq][:], AF.Exp,
                         r=[("ps", pq)], w=[(pf, "es", kc), ("ps", pq)])
            for qs in range(NSUB):
                gq = qb * NSUB + qs
                i0, i1 = 2 + 2 * (qs % 2), 3 + 2 * (qs % 2)
                for kc in range(TS):
                    l = es(kc)[:, qs * 128:(qs + 1) * 128]
                    self.mm(self.PS[i0][:], l, xht(kc)[:, 0:512],
                            start=(kc == 0), stop=(kc == TS - 1),
                            r=[(pf, "es", kc), (pf, "xht", kc)], w=[("ps", i0)])
                    self.mm(self.PS[i1][:], l, xht(kc)[:, 512:1024],
                            start=(kc == 0), stop=(kc == TS - 1),
                            r=[(pf, "es", kc), (pf, "xht", kc)], w=[("ps", i1)])
                    self.mm(self.PS[6][:, 0:1], l, self.ONES1[:],
                            start=(kc == 0), stop=(kc == TS - 1),
                            r=[(pf, "es", kc), ("ones1",)], w=[("ps", 6)])
                self.recip(self.RINV[:, qs:qs + 1], self.PS[6][:, 0:1],
                           r=[("ps", 6)], w=[(pf, "rinv", qs), ("ps", 6)])
                pa = qs % 2
                for half, pi in ((0, i0), (1, i1)):
                    self.stt(AO[pa][:, half * 512:(half + 1) * 512],
                             self.PS[pi][:], self.RINV[:, qs:qs + 1],
                             xht(gq)[:, half * 512:(half + 1) * 512],
                             ALU.mult, ALU.add,
                             r=[("ps", pi), (pf, "rinv", qs), (pf, "xht", gq)],
                             w=[(pf, "ao", pa), ("ps", pi)])
                self.ln_stats(AO[pa][:], qs, [(pf, "ao", pa)], pf)
                self.ln_math(qs, 1, pf)
                for half in range(2):
                    self.ts(TMP[half], AO[pa][:, half * 512:(half + 1) * 512],
                            self.NEGM[:, qs:qs + 1], self.RSTD[:, qs:qs + 1],
                            ALU.add, ALU.mult,
                            r=[(pf, "ao", pa), (pf, "negm"), (pf, "rstd")],
                            w=[(pf, "tmp", half)])
                    self.tt(self.XRES[:, gq * D + half * 512:gq * D + (half + 1) * 512],
                            self.XRES[:, gq * D + half * 512:gq * D + (half + 1) * 512],
                            TMP[half], ALU.add,
                            r=[(pf, "tmp", half)], w=[("xres", gq)])

    def wkt_block(self, dcb):
        return self.wkt_p[dcb]

    def phaseD(self, s):
        pf = ("D", s)
        XFD = self.view(BF16, FC * 544)
        XR = self.view(BF16, FC * 512)
        XK = self.view(BF16, FC * 512)
        KACT = self.view(BF16, FC * 512)
        RT = self.view(BF16, NSUB * D)
        VR = self.view(F32, NSUB * D)
        WS2 = self.view(BF16, FC * 512)
        XH = [self.view(BF16, D) for _ in range(2)]
        DD = self.view(BF16, 512)
        KR = self.view(BF16, 512)
        XO = [self.view(F32, D) for _ in range(2)]
        TMP = [self.view(F32, 512) for _ in range(2)]
        xfd3 = XFD.rearrange("p (fc r) -> p fc r", fc=FC)
        xfd = lambda fc: XFD[:, fc * 544:(fc + 1) * 544]
        xr = lambda fc: XR[:, fc * 512:(fc + 1) * 512]
        xk = lambda fc: XK[:, fc * 512:(fc + 1) * 512]
        ka = lambda dcb: KACT[:, dcb * 512:(dcb + 1) * 512]

        self.vmemset(xfd3[:, :, 15:16], 0.0, w=[(pf, "bnd")])

        for c in range(NCH):
            xfdr = [(pf, "xfd", fc, sub) for fc in range(FC) for sub in range(NSUB)]
            if c > 0:
                self.vcopy(xfd3[:, :, 15:16], xfd3[:, :, 527:528],
                           r=xfdr, w=[(pf, "bnd")])
            for sub in range(NSUB):
                g = c * NSUB + sub
                self.ln_stats(self.XRES[:, g * D:(g + 1) * D], sub, [("xres", g)], pf)
            self.ln_math(0, NSUB, pf)
            for sub in range(NSUB):
                g = c * NSUB + sub
                p = sub % 2
                self.ts(XH[p], self.XRES[:, g * D:(g + 1) * D],
                        self.NEGM[:, sub:sub + 1], self.RSTD[:, sub:sub + 1],
                        ALU.add, ALU.mult,
                        r=[("xres", g), (pf, "negm"), (pf, "rstd")], w=[(pf, "xh", p)])
                for fc in range(FC):
                    self.dmat(xfd(fc)[:, 16 + sub * 128:16 + (sub + 1) * 128],
                              XH[p][:, fc * 128:(fc + 1) * 128],
                              r=[(pf, "xh", p), (pf, "bnd")],
                              w=[(pf, "xfd", fc, sub)])
            for fc in range(FC):
                cur = xfd(fc)[:, 16:528]
                prv = xfd(fc)[:, 15:527]
                self.tt(DD, cur, prv, ALU.subtract,
                        r=xfdr + [(pf, "bnd")], w=[(pf, "dd")])
                self.stt(xr(fc), DD, self.MUR[:, fc:fc + 1], prv, ALU.mult, ALU.add,
                         r=[(pf, "dd"), ("mur",)] + xfdr + [(pf, "bnd")],
                         w=[(pf, "xr", fc)])
                self.stt(xk(fc), DD, self.MUK[:, fc:fc + 1], prv, ALU.mult, ALU.add,
                         r=[(pf, "dd"), ("muk",)] + xfdr + [(pf, "bnd")],
                         w=[(pf, "xk", fc)])
            xrr = [(pf, "xr", fc) for fc in range(FC)]
            xkr = [(pf, "xk", fc) for fc in range(FC)]
            for dcb in range(FC):
                p = dcb % 2
                wb = self.WSTR[:, p * 1024:p * 1024 + 1024]
                self.dma("sync", wb, self.wkt_block(dcb), f"wb{p}",
                         r=[("wkt_p", j) for j in range(4)], w=[(pf, "wbd", p)])
                for fc in range(FC):
                    self.mm(self.PS[p][:], wb[:, fc * 128:(fc + 1) * 128], xk(fc),
                            start=(fc == 0), stop=(fc == FC - 1),
                            r=[(pf, "wbd", p)] + xkr, w=[("ps", p)])
                self.act(KR, self.PS[p][:], AF.Relu,
                         r=[("ps", p)], w=[(pf, "kr"), ("ps", p)])
                self.tt(ka(dcb), KR, KR, ALU.mult, r=[(pf, "kr")], w=[(pf, "ka", dcb)])
            for half in range(2):
                for fc in range(FC):
                    self.dma("sync", WS2[:, fc * 512:(fc + 1) * 512],
                             self.wrt_p[fc * 128:(fc + 1) * 128,
                                        half * 512:(half + 1) * 512],
                             "ws2",
                             r=[("wrt_p", fc // 2)], w=[(pf, "ws2d", fc)])
                for sub in range(NSUB):
                    for fc in range(FC):
                        self.mm(self.PS[2 + sub][:],
                                xr(fc)[:, sub * 128:(sub + 1) * 128],
                                WS2[:, fc * 512:(fc + 1) * 512],
                                start=(fc == 0), stop=(fc == FC - 1),
                                r=[(pf, "ws2d", fc)] + xrr, w=[("ps", 2 + sub)])
                    self.act(RT[:, sub * D + half * 512:sub * D + (half + 1) * 512],
                             self.PS[2 + sub][:], AF.Sigmoid,
                             r=[("ps", 2 + sub)],
                             w=[(pf, "rt", sub), ("ps", 2 + sub)])
            for half in range(2):
                for dcb in range(FC):
                    self.dma("sync", WS2[:, dcb * 512:(dcb + 1) * 512],
                             self.wvt_p[dcb * 128:(dcb + 1) * 128,
                                        half * 512:(half + 1) * 512],
                             "ws2",
                             r=[("wvt_p", dcb // 2)], w=[(pf, "ws2d", dcb)])
                for sub in range(NSUB):
                    for dcb in range(FC):
                        self.mm(self.PS[2 + sub][:],
                                ka(dcb)[:, sub * 128:(sub + 1) * 128],
                                WS2[:, dcb * 512:(dcb + 1) * 512],
                                start=(dcb == 0), stop=(dcb == FC - 1),
                                r=[(pf, "ws2d", dcb), (pf, "ka", dcb)],
                                w=[("ps", 2 + sub)])
                    self.tt(VR[:, sub * D + half * 512:sub * D + (half + 1) * 512],
                            self.PS[2 + sub][:],
                            RT[:, sub * D + half * 512:sub * D + (half + 1) * 512],
                            ALU.mult,
                            r=[("ps", 2 + sub), (pf, "rt", sub)],
                            w=[(pf, "vr", sub), ("ps", 2 + sub)])
            for sub in range(NSUB):
                self.ln_stats(VR[:, sub * D:(sub + 1) * D], sub, [(pf, "vr", sub)], pf)
            self.ln_math(0, NSUB, pf)
            for sub in range(NSUB):
                g = c * NSUB + sub
                p = sub % 2
                for half in range(2):
                    self.ts(TMP[half],
                            VR[:, sub * D + half * 512:sub * D + (half + 1) * 512],
                            self.NEGM[:, sub:sub + 1], self.RSTD[:, sub:sub + 1],
                            ALU.add, ALU.mult,
                            r=[(pf, "vr", sub), (pf, "negm"), (pf, "rstd")],
                            w=[(pf, "tmp", half)])
                    self.tt(XO[p][:, half * 512:(half + 1) * 512],
                            self.XRES[:, g * D + half * 512:g * D + (half + 1) * 512],
                            TMP[half], ALU.add,
                            r=[(pf, "tmp", half), ("xres", g)], w=[(pf, "xo", p)])
                self.dma("sync", self.out[s, g * 128:(g + 1) * 128, :], XO[p][:],
                         "out", r=[(pf, "xo", p)], w=[("out", s, g)])


def _layouts(i):
    def col16(v):
        return np.ascontiguousarray(v.reshape(16, 128).T)
    out = {}
    out["conv_w_r"] = np.ascontiguousarray(
        i["L0_attn_conv_w"].reshape(16, 128, 4).transpose(1, 0, 2).reshape(128, 64))
    out["conv_b_r"] = col16(i["L0_attn_conv_b"])
    out["b_dt_r"] = col16(i["L0_attn_b_dt"])
    out["d_r"] = col16(i["L0_attn_D"])
    out["a_log_r"] = np.ascontiguousarray(
        i["L0_attn_A_log"].reshape(16, 128, 16).transpose(1, 0, 2).reshape(128, 256))
    out["w_in"] = np.ascontiguousarray(
        i["L0_attn_W_in"].reshape(8, 128, 32, 128).transpose(2, 1, 0, 3)
        .reshape(32, 128, 1024))
    out["w_x"] = np.ascontiguousarray(
        i["L0_attn_W_x"].reshape(16, 128, 96).transpose(1, 0, 2).reshape(128, 16 * 96))
    out["w_dt"] = np.ascontiguousarray(i["L0_attn_W_dt"])
    out["w_out"] = np.ascontiguousarray(i["L0_attn_W_out"])
    out["w0t"] = np.ascontiguousarray(i["L0_ffn_weight"].T)
    out["gatet"] = np.ascontiguousarray(
        i["L0_ffn_gate_w"].T.reshape(8, 128, 8).transpose(1, 0, 2).reshape(128, 64))
    out["ut"] = np.ascontiguousarray(
        i["L0_ffn_lora_U"].reshape(128, 1024).T.reshape(8, 128, 128)
        .transpose(1, 0, 2).reshape(128, 8 * 128))
    out["v_r"] = np.ascontiguousarray(
        i["L0_ffn_lora_V"].transpose(0, 2, 1).reshape(128, 1024))
    out["wrt"] = np.ascontiguousarray(i["L1_ffn_W_r"].T)
    out["wkt"] = np.ascontiguousarray(
        i["L1_ffn_W_k"].T.reshape(8, 128, 8, 128).transpose(2, 1, 0, 3)
        .reshape(8, 128, 1024))
    out["wvt"] = np.ascontiguousarray(i["L1_ffn_W_v"].T)
    out["mur_r"] = np.ascontiguousarray(i["L1_ffn_mu_r"].reshape(8, 128).T)
    out["muk_r"] = np.ascontiguousarray(i["L1_ffn_mu_k"].reshape(8, 128).T)
    bf = ml_dtypes.bfloat16
    selb = np.zeros((32, NST * 128), np.float32)
    selc = np.zeros((32, NST * 128), np.float32)
    for n in range(NST):
        selb[n, n * 128:(n + 1) * 128] = 1.0
        selc[16 + n, n * 128:(n + 1) * 128] = 1.0
    out["selb"] = selb.astype(bf)
    out["selc"] = selc.astype(bf)
    rep = np.zeros((8, 128), np.float32)
    for e in range(8):
        rep[e, e * 16:(e + 1) * 16] = 1.0
    out["rep8"] = rep.astype(bf)
    out["ones1"] = np.ones((128, 1), np.float32).astype(bf)
    out["iden"] = np.eye(128, dtype=np.float32).astype(bf)
    return out


_NC_CACHE = None


_BUILDER = None


def _get_nc():
    global _NC_CACHE, _BUILDER
    if _NC_CACHE is None:
        _BUILDER = Builder()
        _NC_CACHE = _BUILDER.build()
    return _NC_CACHE


def kernel(**inputs):
    x = np.ascontiguousarray(np.asarray(inputs["L0_attn_x"], dtype=np.float32))
    shared = _layouts({k: np.asarray(v, dtype=np.float32)
                       for k, v in inputs.items() if k != "L0_attn_x"})
    nc = _get_nc()
    in_maps = []
    for i in range(8):
        m = dict(shared)
        m["x"] = np.ascontiguousarray(x[2 * i:2 * i + 2])
        in_maps.append(m)
    res = run_bass_kernel_spmd(nc, in_maps, core_ids=list(range(8)))
    out = np.stack([r["out"] for r in res.results], axis=0)
    return out.reshape(16, SEQ, D).astype(np.float32)
